# version 02
# baseline (speedup 1.0000x reference)
import contextlib
import numpy as np
import ml_dtypes
import concourse.bass as bass
import concourse.mybir as mybir
from concourse.bass_utils import run_bass_kernel_spmd

F32 = mybir.dt.float32
BF16 = mybir.dt.bfloat16
AF = mybir.ActivationFunctionType
ALU = mybir.AluOpType
AX = mybir.AxisListType
BF = ml_dtypes.bfloat16

T = 2048
D = 2048
NT = 16
CH = 128
NCH = T // CH
P3_CHUNKS = NCH
NKH = 16
NVH = 32
QKVN = 8192
ZN = 4096
FFI = 8192

ENGS = ("sync", "tensor", "vector", "scalar", "gpsimd")
qS, qP, qG = "qS", "qP", "qG"
eT, eV, eS, eG = "eT", "eV", "eS", "eG"

(C_INV_D, C_EPS_LN, C_NEGHALF, C_INV_DV, C_EPS_RMS, C_ONE, C_L2EPS, C_QTR,
 C_NEGINV_D, C_NEGONE) = range(10)
NCST = 10


class Prog:
    selfsync = False

    def __init__(self, nc, sem_names):
        self.nc = nc
        self._stk = contextlib.ExitStack()
        self.sem = {n: self._stk.enter_context(nc.semaphore(n))
                    for n in sem_names}
        self.tot = {n: 0 for n in sem_names}
        self.ops = {e: [] for e in ENGS}
        self.dma_sems = set()
        self._last_mark = {}

    def _sem(self, n):
        if n not in self.sem:
            self.sem[n] = self._stk.enter_context(self.nc.semaphore(n))
            self.tot[n] = 0
        return n

    ENG_OF = {"eT": "tensor", "eV": "vector", "eS": "scalar",
              "eG": "gpsimd", "qS": "sync", "qP": "scalar", "qG": "gpsimd"}

    def do(self, eng, fn, waits=(), inc=None, n=1):
        eng = self.ENG_OF.get(eng, eng)
        if self.selfsync and eng in ("tensor", "vector", "scalar", "gpsimd"):
            last = self._last_mark.get(eng)
            waits = list(waits) + ([last] if last else [])
            if inc is None:
                inc, n = self._sem("ss_" + eng), 1
            self.ops[eng].append([tuple(waits), fn, inc, n])
            self.tot[inc] += n
            self._last_mark[eng] = (inc, self.tot[inc])
            return (inc, self.tot[inc])
        self.ops[eng].append([tuple(waits), fn, inc, n])
        if inc is not None:
            self.tot[inc] += n
            return (inc, self.tot[inc])
        return None

    def dma(self, eng, out, in_, sem, waits=(), **kw):
        self._sem(sem)
        self.dma_sems.add(sem)
        w = list(waits)
        if self.tot[sem] > 0:
            w.append((sem, self.tot[sem]))
        return self.do(eng, lambda e: e.dma_start(out=out, in_=in_, **kw),
                       w, sem, 16)

    def dmat(self, eng, out, in_, sem, waits=()):
        self._sem(sem)
        self.dma_sems.add(sem)
        w = list(waits)
        if self.tot[sem] > 0:
            w.append((sem, self.tot[sem]))
        return self.do(eng, lambda e: e.dma_start_transpose(out, in_),
                       w, sem, 16)

    def drain_all(self):
        waits = [(s, self.tot[s]) for s in sorted(self.dma_sems)
                 if self.tot[s] > 0]
        self.ops["sync"].append([tuple(waits), None, None, 0])

    def barrier(self):
        marks = {}
        for e in ENGS:
            sem = "b_" + e
            last = None
            for item in reversed(self.ops[e]):
                if item[1] is not None:
                    last = item
                    break
            if last is None:
                self.tot[sem] += 1
                marks[e] = (sem, self.tot[sem])
                self.ops[e].append([(), lambda eng: eng.nop(), sem, 1])
            elif last[2] is None:
                last[2] = sem
                last[3] = 1
                self.tot[sem] += 1
                marks[e] = (sem, self.tot[sem])
            else:
                marks[e] = (last[2], self.tot[last[2]])
        dma_waits = tuple((s, self.tot[s]) for s in sorted(self.dma_sems)
                          if self.tot[s] > 0)
        for e in ENGS:
            waits = tuple(marks[o] for o in ENGS if o != e) + dma_waits
            self.ops[e].append([waits, None, None, 0])

    def replay(self, block):
        for eng in ENGS:
            ops = self.ops[eng]
            sems = self.sem

            def make(ops=ops):
                def run(e):
                    last = {}
                    for waits, fn, inc, n in ops:
                        for s, v in waits:
                            if last.get(s, -1) < v:
                                e.wait_ge(sems[s], v)
                                last[s] = v
                        if fn is None:
                            continue
                        ins = fn(e)
                        if inc is not None:
                            ins.then_inc(sems[inc], n)
                return run
            getattr(block, eng)(make())

    def close(self):
        self._stk.close()


class Rot:

    def __init__(self, n):
        self.marks = [()] * n

    def use(self, i):
        return list(self.marks[i % len(self.marks)])

    def mark(self, i, *marks):
        self.marks[i % len(self.marks)] = tuple(m for m in marks
                                                if m is not None)


def build_program(n_phases=99, dbg=()):
    nc = bass.Bass("TRN2", target_bir_lowering=False)
    dt = nc.dram_tensor

    x_in = dt("x", [T, D], F32, kind="ExternalInput").ap()
    c_pf = dt("c_pf", [128, 16], F32, kind="ExternalInput").ap()
    wmod = dt("wmod", [D, 12288], F32, kind="ExternalInput").ap()
    wqkvT = dt("wqkvT", [D, QKVN], F32, kind="ExternalInput").ap()
    wzT = dt("wzT", [D, ZN], F32, kind="ExternalInput").ap()
    wbaT = dt("wbaT", [D, 64], F32, kind="ExternalInput").ap()
    convw = dt("convw", [128, 256], F32, kind="ExternalInput").ap()
    dtb_al = dt("dtb_al", [1, 64], F32, kind="ExternalInput").ap()
    normw_in = dt("normw", [1, 128], F32, kind="ExternalInput").ap()
    woutT = dt("woutT", [ZN, D], F32, kind="ExternalInput").ap()
    wgate = dt("wgate", [D, FFI], F32, kind="ExternalInput").ap()
    wup = dt("wup", [D, FFI], F32, kind="ExternalInput").ap()
    wdown = dt("wdown", [FFI, D], F32, kind="ExternalInput").ap()
    cst_in = dt("cst", [128, NCST], F32, kind="ExternalInput").ap()
    identb_in = dt("identb", [128, 128], BF16, kind="ExternalInput").ap()
    identf_in = dt("identf", [128, 128], F32, kind="ExternalInput").ap()
    ones1b_in = dt("ones1b", [1, 128], BF16, kind="ExternalInput").ap()
    ones32_in = dt("ones32", [32, 128], F32, kind="ExternalInput").ap()
    eye32_in = dt("eye32", [128, 1024], BF16, kind="ExternalInput").ap()
    maskw_in = dt("maskw", [128, 384], BF16, kind="ExternalInput").ap()
    gmask_in = dt("gmask", [128, 384], BF16, kind="ExternalInput").ap()
    sel32_in = dt("sel32", [32, 32 * 128], BF16, kind="ExternalInput").ap()

    out = dt("out", [T, D], F32, kind="ExternalOutput").ap()

    s_qk = dt("s_qk", [2, NKH, 128, T], BF16, kind="Internal").ap()
    s_v = dt("s_v", [NVH, 128, T], BF16, kind="Internal").ap()
    s_zs = dt("s_zs", [T, ZN], BF16, kind="Internal").ap()
    s_gated = dt("s_gated", [T, ZN], BF16, kind="Internal").ap()
    s_h2 = dt("s_h2", [T, D], BF16, kind="Internal").ap()
    s_x2 = dt("s_x2", [T, D], F32, kind="Internal").ap()
    s_h1s = dt("s_h1s", [T, FFI], BF16, kind="Internal").ap()
    s_wg = dt("s_wg", [D, FFI], BF16, kind="Internal").ap()
    s_wu = dt("s_wu", [D, FFI], BF16, kind="Internal").ap()
    s_wd = dt("s_wd", [FFI, D], BF16, kind="Internal").ap()
    s_mod = dt("s_mod", [2, 12288], BF16, kind="Internal").ap()
    s_dum = dt("s_dum", [T, D], F32, kind="Internal").ap()

    dbg_t = {}
    if "h_fm" in dbg:
        dbg_t["h_fm"] = dt("dbg_h_fm", [128, 16 * T], BF16,
                           kind="ExternalOutput").ap()
    if "ba" in dbg:
        dbg_t["ba"] = dt("dbg_ba", [128, 4 * T], F32,
                         kind="ExternalOutput").ap()
    if "mod" in dbg:
        dbg_t["mod"] = dt("dbg_mod", [128, 6 * D], F32,
                          kind="ExternalOutput").ap()
    if "rn" in dbg:
        dbg_t["rn"] = dt("dbg_rn", [32, T], F32,
                         kind="ExternalOutput").ap()
    if "x2h2" in dbg:
        dbg_t["x2"] = dt("dbg_x2", [T, D], F32, kind="ExternalOutput").ap()
        dbg_t["h2"] = dt("dbg_h2", [T, D], BF16, kind="ExternalOutput").ap()
        dbg_t["h1s"] = dt("dbg_h1s", [T, FFI], BF16,
                          kind="ExternalOutput").ap()
    if "m3" in dbg:
        dbg_t["m3"] = dt("dbg_m3", [128, 3 * D], BF16,
                         kind="ExternalOutput").ap()
        dbg_t["smod"] = dt("dbg_smod", [2, 12288], BF16,
                           kind="ExternalOutput").ap()
    if "m3e" in dbg:
        dbg_t["m3e"] = dt("dbg_m3e", [128, 3 * D], BF16,
                          kind="ExternalOutput").ap()
        dbg_t["x2e"] = dt("dbg_x2e", [T, D], F32,
                          kind="ExternalOutput").ap()
    if "gated" in dbg:
        dbg_t["gated"] = dt("dbg_gated", [T, ZN], BF16,
                            kind="ExternalOutput").ap()
        dbg_t["mfin"] = dt("dbg_mfin", [128, NVH * 128], BF16,
                           kind="ExternalOutput").ap()
    if "p0i" in dbg:
        dbg_t["p0i"] = dt("dbg_p0i", [128, 12304], F32,
                          kind="ExternalOutput").ap()
    if "wm0" in dbg:
        dbg_t["wm0"] = dt("dbg_wm0", [128, 16 * 512], BF16,
                          kind="ExternalOutput").ap()

    sems = [qS, qP, qG, eT, eV, eS, eG] + ["b_" + e for e in ENGS]
    p = Prog(nc, sems)
    stk = contextlib.ExitStack()

    def sb(name, shape, dtype):
        return stk.enter_context(nc.sbuf_tensor(name, shape, dtype))

    cst = sb("cst_s", [128, NCST], F32)
    identb = sb("identb_s", [128, 128], BF16)
    identf = sb("identf_s", [128, 128], F32)
    ones1b = sb("ones1b_s", [1, 128], BF16)
    ones32 = sb("ones32_s", [32, 128], F32)
    eye32 = sb("eye32_s", [128, 1024], BF16)
    maskw = sb("maskw_s", [128, 384], BF16)
    gmask = sb("gmask_s", [128, 384], BF16)
    sel32 = sb("sel32_s", [32, 32 * 128], BF16)
    beta_all = sb("beta_all", [128, NT * 32], F32)
    ldrT = sb("ldrT", [32, T], F32)
    lbnrT = sb("lbnrT", [32, T], F32)
    rn_hi = sb("rn_hi", [32, T], BF16)
    rn_lo = sb("rn_lo", [32, T], BF16)
    nwrep = sb("nwrep", [128, ZN], BF16)
    dtbA = sb("dtbA", [128, 64], F32)

    for t_, src in ((cst, cst_in), (identb, identb_in), (identf, identf_in),
                    (ones1b, ones1b_in), (ones32, ones32_in), (eye32, eye32_in),
                    (maskw, maskw_in), (gmask, gmask_in),
                    (sel32, sel32_in)):
        p.dma("sync", t_[:], src, "c_ld")
    w_c0 = ("c_ld", p.tot["c_ld"])

    with contextlib.ExitStack() as ph:
        def psb(n_, s_, d_):
            return ph.enter_context(nc.sbuf_tensor(n_, s_, d_))

        def pps(n_, s_, d_=F32):
            return ph.enter_context(nc.psum_tensor(n_, s_, d_))

        ba_sm = psb("ba_sm", [1, 64], F32)
        tmp2 = psb("tmp2", [1, 64], F32)
        tmp2b = psb("tmp2b", [1, 128], BF16)
        tmp2f = psb("tmp2f", [1, 64], F32)
        nwb = psb("nwb", [1, 128], BF16)
        ps_b0 = pps("ps_b0", [128, 512])
        ps_nw = pps("ps_nw", [128, 512])

        mba = p.dma("sync", ba_sm[:], dtb_al, "ba_ld")
        mnw = p.dma("gpsimd", nwb[:], normw_in, "nw_ld")
        ms = p.do(eS, lambda e: e.activation(tmp2[0:1, 32:64],
                                             ba_sm[0:1, 32:64],
                                             AF.Exp), [mba], eS)
        m = p.do(eV, lambda e: e.tensor_copy(tmp2[0:1, 0:32],
                                             ba_sm[0:1, 0:32]), [mba], eV)
        m = p.do(eV, lambda e: e.tensor_copy(tmp2b[0:1, 0:64], tmp2[0:1, :]),
                 [ms], eV)
        m = p.do(eV, lambda e: e.tensor_sub(tmp2f[0:1, :], tmp2[0:1, :],
                                            tmp2b[0:1, 0:64]), (), eV)
        m = p.do(eV, lambda e: e.tensor_copy(tmp2b[0:1, 64:128], tmp2f[0:1, :]),
                 (), eV)
        t_ = p.do(eT, lambda e: e.matmul(ps_b0[:, 0:64], ones1b[0:1, :],
                                         tmp2b[0:1, 0:64], start=True,
                                         stop=False), [m, w_c0], eT)
        t_ = p.do(eT, lambda e: e.matmul(ps_b0[:, 0:64], ones1b[0:1, :],
                                         tmp2b[0:1, 64:128], start=False,
                                         stop=True), (), eT)
        p.do(eV, lambda e: e.tensor_copy(dtbA[:], ps_b0[:, 0:64]), [t_], eV)

        t_ = p.do(eT, lambda e: e.matmul(ps_nw[:, 0:128], ones1b[0:1, :],
                                         nwb[0:1, :], start=True, stop=True),
                  [mnw, w_c0], eT)
        m = p.do(eV, lambda e: e.tensor_copy(nwrep[:, 0:128], ps_nw[:, 0:128]),
                 [t_], eV)
        for r in range(1, 32):
            p.do(eV, lambda e, r=r: e.tensor_copy(
                nwrep[:, r * 128:(r + 1) * 128], nwrep[:, 0:128]), (), eV)

        cfm = psb("cfm", [128, 16], F32)
        scb = psb("scb", [128, 16], BF16)
        ctmp = psb("ctmp", [128, 16], F32)
        ctmp2 = psb("ctmp2", [128, 16], F32)
        wm_sl = [psb(f"wm_sl{i}", [128, 16 * 512], BF16) for i in range(2)]
        ps_mod = [pps(f"ps_mod{i}", [128, 512]) for i in range(2)]
        modrow = psb("modrow", [1, 12288], F32)
        segbuf = psb("segbuf", [1, 512], F32)
        modhi = psb("modhi", [1, 12288], BF16)
        modlo = psb("modlo", [1, 12288], BF16)
        clA = psb("clA", [1, 512], F32)
        clB = psb("clB", [1, 512], F32)

        mc = p.dma("sync", cfm[:], c_pf, "cpf_ld")
        m = p.do(eS, lambda e: e.activation(ctmp[:], cfm[:], AF.Exp,
                                            scale=-1.0), [mc], eS)
        m = p.do(eS, lambda e: e.activation(ctmp[:], ctmp[:], AF.Ln,
                                            bias=1.0), (), eS)
        m = p.do(eS, lambda e: e.activation(ctmp2[:], ctmp[:], AF.Exp,
                                            scale=-1.0), (), eS)
        msc = p.do(eV, lambda e: e.tensor_mul(scb[:], ctmp2[:], cfm[:]),
                   [m], eV)

        wm4 = wmod.rearrange("(ch p) n -> p ch n", p=128)
        rb_wm = Rot(2)
        rb_pm = Rot(2)
        for nb in range(24):
            buf = wm_sl[nb % 2]
            ld = p.dma("gpsimd", buf[:], wm4[:, :, nb * 512:(nb + 1) * 512],
                       f"wm{nb % 2}", waits=rb_wm.use(nb))
            pb = ps_mod[nb % 2]
            if nb == 0 and "wm0" in dbg:
                wdump = p.dma("scalar", dbg_t["wm0"][:, 0:16], scb[:], "dbg",
                              waits=[ld, msc])
            else:
                wdump = None
            for chk in range(16):
                w2 = ([ld, msc] + rb_pm.use(nb)
                      + ([wdump] if wdump else [])) if chk == 0 else ()
                p.do(eT, lambda e, pb=pb, chk=chk, buf=buf: e.matmul(
                    pb[0:1, :], scb[:, chk:chk + 1],
                    buf[:, chk * 512:(chk + 1) * 512],
                    start=(chk == 0), stop=(chk == 15)), w2, eT)
            mm = (eT, p.tot[eT])
            rb_wm.mark(nb, mm)
            ev = p.do(eV, lambda e, pb=pb, nb=nb: e.tensor_copy(
                modrow[0:1, nb * 512:(nb + 1) * 512], pb[0:1, :]), [mm], eV)
            rb_pm.mark(nb, ev)
        mod_raw = (eV, p.tot[eV])

        clA2 = psb("clA2", [1, 512], F32)
        clB2 = psb("clB2", [1, 512], F32)
        for seg6 in range(24):
            seg = seg6 // 4
            off = seg6 * 512
            sl_ = modrow[0:1, off:off + 512]
            if seg in (2, 5):
                p.do(eV, lambda e, sl_=sl_: e.tensor_single_scalar(
                    clA[:], sl_, 0.0, ALU.is_ge), [mod_raw], eV)
                p.do(eV, lambda e: e.tensor_scalar(
                    clA2[:], clA[:], 2.0, 1.0, ALU.mult, ALU.subtract),
                    (), eV)
                p.do(eV, lambda e, sl_=sl_: e.tensor_scalar_mul(
                    clB[:], sl_, -1.0), (), eV)
                p.do(eV, lambda e, sl_=sl_: e.tensor_max(clB2[:], clB[:],
                                                         sl_), (), eV)
                p.do(eV, lambda e: e.tensor_scalar_max(clB[:], clB2[:],
                                                       0.25), (), eV)
                p.do(eV, lambda e: e.tensor_mul(segbuf[:], clA2[:], clB[:]),
                     (), eV)
            elif seg in (1, 4):
                p.do(eV, lambda e, sl_=sl_: e.tensor_scalar_add(
                    segbuf[:], sl_, 1.0), [mod_raw], eV)
            else:
                p.do(eV, lambda e, sl_=sl_: e.tensor_copy(segbuf[:], sl_),
                     [mod_raw], eV)
            p.do(eV, lambda e, off=off: e.tensor_copy(
                modhi[0:1, off:off + 512], segbuf[:]), (), eV)
            p.do(eV, lambda e, off=off: e.tensor_sub(
                clA[:], segbuf[:], modhi[0:1, off:off + 512]), (), eV)
            p.do(eV, lambda e, off=off: e.tensor_copy(
                modlo[0:1, off:off + 512], clA[:]), (), eV)
        if "p0i" in dbg:
            mraw = (eV, p.tot[eV])
            p.dma("scalar", dbg_t["p0i"][0:1, 0:12288], modrow[:], "dbg",
                  waits=[mraw])
            p.dma("scalar", dbg_t["p0i"][:, 12288:12304], ctmp2[:], "dbg",
                  waits=[mraw])
        mmod = (eV, p.tot[eV])
        p.dma("scalar", s_mod[0:1, :], modhi[:], "mst0", waits=[mmod])
        p.dma("scalar", s_mod[1:2, :], modlo[:], "mst1", waits=[mmod])

        if "mod" in dbg:
            pdm = pps("pdm", [128, 512])
            dmp = psb("dmp", [128, 512], F32)
            mv = (eV, p.tot[eV])
            prev = None
            for seg in range(6):
                for blk in range(4):
                    off = seg * D + blk * 512
                    w0 = [mv] + ([prev] if prev else [])
                    p.do(eT, lambda e, off=off: e.matmul(
                        pdm[:], ones1b[0:1, :], modhi[0:1, off:off + 512],
                        start=True, stop=False), w0, eT)
                    tm = p.do(eT, lambda e, off=off: e.matmul(
                        pdm[:], ones1b[0:1, :], modlo[0:1, off:off + 512],
                        start=False, stop=True), (), eT)
                    vv = p.do(eV, lambda e: e.tensor_copy(dmp[:], pdm[:]),
                              [tm], eV)
                    st = p.dma("scalar", dbg_t["mod"][:, off:off + 512],
                               dmp[:], "dbg", waits=[vv])
                    prev = st

    p.barrier()

    ph12 = contextlib.ExitStack()
    h_fm = ph12.enter_context(nc.sbuf_tensor("h_fm", [128, 16 * T], BF16))
    with contextlib.ExitStack() as ph:
        def psb(n_, s_, d_):
            return ph.enter_context(nc.sbuf_tensor(n_, s_, d_))

        def pps(n_, s_, d_=F32):
            return ph.enter_context(nc.psum_tensor(n_, s_, d_))

        xt = [psb(f"xt{i}", [128, D], F32) for i in range(2)]
        ca1t = psb("ca1t", [128, D], F32)
        sat = psb("sat", [128, D], F32)
        wba_sb = psb("wba_sb", [128, 16 * 64], BF16)
        stat = [psb(f"stat{i}", [128, 6], F32) for i in range(2)]
        h1t = psb("h1t", [128, D], BF16)
        sqd = psb("sqd", [128, D], F32)
        h2t = [psb(f"h2t{i}", [128, D], BF16) for i in range(2)]
        htok = [psb(f"htok{i}", [128, D], BF16) for i in range(2)]
        braw_all = psb("braw_all", [128, 512], F32)
        araw_all = psb("araw_all", [128, 512], F32)
        lbn_w = psb("lbn_w", [128, 512], F32)
        yw = psb("yw", [128, 512], F32)
        ayw = psb("ayw", [128, 512], F32)
        wtmp = psb("wtmp", [128, 512], F32)
        wtmp2 = psb("wtmp2", [128, 512], F32)
        dtb_rep = psb("dtb_rep", [128, 512], F32)
        eA_rep = psb("eA_rep", [128, 512], F32)
        ps_tr = [pps(f"ps_tr{i}", [128, 128], BF16) for i in range(2)]
        ps_ba = [pps(f"ps_ba{i}", [128, 64]) for i in range(2)]
        ps_rt = [pps(f"ps_rt{i}", [32, 256]) for i in range(2)]
        ps_m = pps("ps_m", [128, 512])
        modh1 = psb("modh1", [1, 2048], BF16)
        modl1 = psb("modl1", [1, 2048], BF16)

        prev = None
        for seg, dst in ((1, ca1t), (0, sat)):
            mld = p.dma("sync", modh1[:], s_mod[0:1, seg * D:(seg + 1) * D],
                        "mseg0", waits=[("mst0", p.tot.get("mst0", 0))]
                        + ([prev] if prev else []))
            mld2 = p.dma("sync", modl1[:], s_mod[1:2, seg * D:(seg + 1) * D],
                         "mseg1", waits=[("mst1", p.tot.get("mst1", 0))]
                         + ([prev] if prev else []))
            for blk in range(4):
                off = blk * 512
                w0 = [prev, mld, mld2] if prev else [mld, mld2]
                p.do(eT, lambda e, off=off: e.matmul(
                    ps_m[:], ones1b[0:1, :], modh1[0:1, off:off + 512],
                    start=True, stop=False), w0, eT)
                tm = p.do(eT, lambda e, off=off: e.matmul(
                    ps_m[:], ones1b[0:1, :], modl1[0:1, off:off + 512],
                    start=False, stop=True), (), eT)
                prev = p.do(eV, lambda e, d=dst, blk=blk: e.tensor_copy(
                    d[:, blk * 512:(blk + 1) * 512], ps_m[:]), [tm], eV)
        vmod = prev

        wba4 = wbaT.rearrange("(ch p) n -> p ch n", p=128)
        mwba = p.dma("gpsimd", wba_sb[:], wba4, "wba_ld")

        rb_x = Rot(2)
        rb_htok = Rot(2)
        rb_h2 = Rot(2)
        rb_ptr = Rot(2)
        rb_pba = Rot(2)
        rb_prt = Rot(2)
        rb_stat = Rot(2)
        for ti in range(NT):
            xb = xt[ti % 2]
            stt_ = stat[ti % 2]
            mx = p.dma("sync", xb[:], x_in[ti * 128:(ti + 1) * 128, :],
                       f"x{ti % 2}", waits=rb_x.use(ti))
            m = p.do(eV, lambda e, xb=xb, s_=stt_: e.reduce_sum(
                s_[:, 0:1], xb[:], AX.X), [mx] + rb_stat.use(ti), eV)
            m = p.do(eS, lambda e, s_=stt_: e.activation(
                s_[:, 4:5], s_[:, 0:1], AF.Identity,
                scale=cst[:, C_NEGINV_D:C_NEGINV_D + 1]), [m], eS)
            sv = p.do(eS, lambda e, xb=xb, s_=stt_: e.activation(
                sqd[:], xb[:], AF.Square, bias=s_[:, 4:5],
                accum_out=s_[:, 1:2]), [m], eS)
            sv = p.do(eS, lambda e, s_=stt_: e.activation(
                s_[:, 2:3], s_[:, 1:2], AF.Ln,
                scale=cst[:, C_INV_D:C_INV_D + 1],
                bias=cst[:, C_EPS_LN:C_EPS_LN + 1]), (), eS)
            sv = p.do(eS, lambda e, s_=stt_: e.activation(
                s_[:, 3:4], s_[:, 2:3], AF.Exp,
                scale=cst[:, C_NEGHALF:C_NEGHALF + 1]), (), eS)
            hv = p.do(eV, lambda e, xb=xb, s_=stt_: e.tensor_scalar(
                h1t[:], xb[:], s_[:, 4:5], s_[:, 3:4], ALU.add, ALU.mult),
                [sv, vmod], eV)
            rb_stat.mark(ti, hv)
            h2b = h2t[ti % 2]
            hv = p.do(eV, lambda e, h2b=h2b: e.tensor_mul(
                h2b[:], h1t[:], ca1t[:]), rb_h2.use(ti), eV)
            hb = htok[ti % 2]
            gv = p.do(eV, lambda e, hb=hb, h2b=h2b: e.tensor_add(
                hb[:], h2b[:], sat[:]), [hv] + rb_htok.use(ti), eV)
            rb_h2.mark(ti, gv)
            rb_x.mark(ti, hv)
            ev_last = None
            for chk in range(16):
                pt = ps_tr[chk % 2]
                w2 = ([gv] if chk == 0 else []) + rb_ptr.use(chk)
                tt = p.do(eT, lambda e, hb=hb, chk=chk, pt=pt: e.transpose(
                    pt[:], hb[:, chk * 128:(chk + 1) * 128], identb[:]),
                    w2, eT)
                ev = p.do(eS, lambda e, pt=pt, chk=chk, ti=ti: e.copy(
                    h_fm[:, chk * T + ti * 128: chk * T + ti * 128 + 128],
                    pt[:]), [tt], eS)
                rb_ptr.mark(chk, ev)
                ev_last = ev
            rb_htok.mark(ti, (eT, p.tot[eT]))
            pb = ps_ba[ti % 2]
            for chk in range(16):
                w3 = ([ev_last, mwba] + rb_pba.use(ti)) if chk == 0 else ()
                p.do(eT, lambda e, pb=pb, chk=chk, ti=ti: e.matmul(
                    pb[:],
                    h_fm[:, chk * T + ti * 128: chk * T + ti * 128 + 128],
                    wba_sb[:, chk * 64:(chk + 1) * 64],
                    start=(chk == 0), stop=(chk == 15)), w3, eT)
            ba_mm = (eT, p.tot[eT])
            c1 = p.do(eS, lambda e, pb=pb, ti=ti: e.copy(
                braw_all[:, ti * 32:(ti + 1) * 32], pb[:, 0:32]),
                [ba_mm], eS)
            c2 = p.do(eS, lambda e, pb=pb, ti=ti: e.copy(
                araw_all[:, ti * 32:(ti + 1) * 32], pb[:, 32:64]), (), eS)
            rb_pba.mark(ti, c2)
        done = (eS, p.tot[eS])
        for rr in range(16):
            p.do(eS, lambda e, rr=rr: e.copy(
                dtb_rep[:, rr * 32:(rr + 1) * 32], dtbA[:, 0:32]), (), eS)
            p.do(eS, lambda e, rr=rr: e.copy(
                eA_rep[:, rr * 32:(rr + 1) * 32], dtbA[:, 32:64]), (), eS)
        mrep = (eS, p.tot[eS])
        m = p.do(eS, lambda e: e.activation(wtmp[:], braw_all[:], AF.Exp,
                                            scale=-1.0), [done], eS)
        m = p.do(eS, lambda e: e.activation(lbn_w[:], wtmp[:], AF.Ln,
                                            bias=1.0), (), eS)
        m = p.do(eS, lambda e: e.activation(beta_all[:], lbn_w[:], AF.Exp,
                                            scale=-1.0), (), eS)
        yv = p.do(eV, lambda e: e.tensor_add(yw[:], araw_all[:], dtb_rep[:]),
                  [done, mrep], eV)
        m2 = p.do(eV, lambda e: e.tensor_scalar_mul(wtmp2[:], yw[:], -1.0),
                  [m], eV)
        m2 = p.do(eV, lambda e: e.tensor_max(ayw[:], wtmp2[:], yw[:]),
                  (), eV)
        m2 = p.do(eS, lambda e: e.activation(wtmp[:], ayw[:], AF.Exp,
                                             scale=-1.0), [m2], eS)
        m2 = p.do(eS, lambda e: e.activation(wtmp[:], wtmp[:], AF.Ln,
                                             bias=1.0), (), eS)
        rv = p.do(eV, lambda e: e.tensor_relu(wtmp2[:], yw[:]), [m2], eV)
        sp = p.do(eV, lambda e: e.tensor_add(ayw[:], wtmp2[:], wtmp[:]),
                  (), eV)
        ldv = p.do(eV, lambda e: e.scalar_tensor_tensor(
            yw[:], ayw[:], -1.0, eA_rep[:], ALU.mult, ALU.mult), (), eV)
        for ti in range(NT):
            prt = ps_rt[ti % 2]
            w1 = [ldv] + rb_prt.use(ti)
            tt1 = p.do(eT, lambda e, ti=ti, prt=prt: e.transpose(
                prt[:, 0:128], yw[:, ti * 32:(ti + 1) * 32], identf[:]),
                w1, eT)
            tt2 = p.do(eT, lambda e, ti=ti, prt=prt: e.transpose(
                prt[:, 128:256], lbn_w[:, ti * 32:(ti + 1) * 32],
                identf[:]), [m], eT)
            cv = p.do(eV, lambda e, ti=ti, prt=prt: e.tensor_copy(
                ldrT[:, ti * 128:(ti + 1) * 128], prt[:, 0:128]),
                [tt2], eV)
            cv = p.do(eV, lambda e, ti=ti, prt=prt: e.tensor_copy(
                lbnrT[:, ti * 128:(ti + 1) * 128], prt[:, 128:256]), (), eV)
            rb_prt.mark(ti, cv)

        if "h_fm" in dbg:
            p.dma("scalar", dbg_t["h_fm"], h_fm[:], "dbg",
                  waits=[(eS, p.tot[eS]), (eV, p.tot[eV])])
        if "ba" in dbg:
            vv = [(eV, p.tot[eV]), (eS, p.tot[eS]), (eT, p.tot[eT])]
            p.dma("scalar", dbg_t["ba"][:, 0:512], beta_all[:], "dbg",
                  waits=vv)
            p.dma("scalar", dbg_t["ba"][0:32, 512:2560], ldrT[:], "dbg",
                  waits=vv)
            p.dma("scalar", dbg_t["ba"][0:32, 2560:4608], lbnrT[:], "dbg",
                  waits=vv)

    p.barrier()
    if n_phases <= 1:
        _finish(p, nc)
        return nc

    with contextlib.ExitStack() as ph:
        def psb(n_, s_, d_):
            return ph.enter_context(nc.sbuf_tensor(n_, s_, d_))

        def pps(n_, s_, d_=F32):
            return ph.enter_context(nc.psum_tensor(n_, s_, d_))

        wz_sl = [psb(f"wz_sl{i}", [128, 16 * 512], BF16) for i in range(2)]
        ps_z = [pps(f"ps_z{i}", [128, 512]) for i in range(2)]
        zs_b = [psb(f"zs_b{i}", [128, 512], BF16) for i in range(2)]
        wz4 = wzT.rearrange("(ch p) n -> p ch n", p=128)
        rb_wz = Rot(2)
        rb_pz = Rot(2)
        rb_zb = Rot(2)
        for ns in range(8):
            buf = wz_sl[ns % 2]
            ld = p.dma("gpsimd", buf[:], wz4[:, :, ns * 512:(ns + 1) * 512],
                       f"wz{ns % 2}", waits=rb_wz.use(ns))
            for ti in range(NT):
                seq = ns * NT + ti
                pz = ps_z[seq % 2]
                for chk in range(16):
                    w2 = ([ld] + rb_pz.use(seq)) if chk == 0 else ()
                    p.do(eT, lambda e, pz=pz, chk=chk, ti=ti, buf=buf:
                         e.matmul(pz[:],
                                  h_fm[:, chk * T + ti * 128:
                                       chk * T + ti * 128 + 128],
                                  buf[:, chk * 512:(chk + 1) * 512],
                                  start=(chk == 0), stop=(chk == 15)),
                         w2, eT)
                zmm = (eT, p.tot[eT])
                zb = zs_b[seq % 2]
                sv = p.do(eS, lambda e, pz=pz, zb=zb: e.activation(
                    zb[:], pz[:], AF.Silu), [zmm] + rb_zb.use(seq), eS)
                rb_pz.mark(seq, sv)
                st = p.dma("scalar",
                           s_zs[ti * 128:(ti + 1) * 128,
                                ns * 512:(ns + 1) * 512], zb[:],
                           f"zst{seq % 2}", waits=[sv])
                rb_zb.mark(seq, st)
            rb_wz.mark(ns, (eT, p.tot[eT]))

        wq_sl = [psb(f"wq_sl{i}", [128, 16 * 128], BF16) for i in range(2)]
        ps_q = [pps(f"ps_q{i}", [128, 512]) for i in range(2)]
        qraw = [psb(f"qraw{i}", [128, T], BF16) for i in range(2)]
        qcb_ = [psb(f"qc{i}", [128, T], BF16) for i in range(2)]
        sqb = [psb(f"sqb{i}", [128, T], BF16) for i in range(2)]
        cw_sb = psb("cw_sb", [128, 256], F32)
        ps_ss = [pps(f"ps_ss{i}", [32, 512]) for i in range(4)]
        ss_all = psb("ss_all", [32, T], F32)
        ss_f = psb("ss_f", [32, T], F32)

        mcw = p.dma("sync", cw_sb[:], convw, "cw_ld")
        wq4 = wqkvT.rearrange("(ch p) n -> p ch n", p=128)
        rb_wq = Rot(2)
        rb_pq = Rot(2)
        rb_qr = Rot(2)
        rb_qc = Rot(2)
        rb_sq = Rot(2)
        rb_ss = Rot(1)
        for sl in range(64):
            buf = wq_sl[sl % 2]
            ld = p.dma("gpsimd", buf[:], wq4[:, :, sl * 128:(sl + 1) * 128],
                       f"wq{sl % 2}", waits=rb_wq.use(sl))
            qr = qraw[sl % 2]
            qcb = qcb_[sl % 2]
            ev_done = []
            for tb in range(4):
                pseq = sl * 4 + tb
                pq = ps_q[pseq % 2]
                for chk in range(16):
                    w2 = ([ld] + rb_pq.use(pseq) +
                          (rb_qr.use(sl) if tb == 0 else [])) \
                        if chk == 0 else ()
                    p.do(eT, lambda e, pq=pq, chk=chk, tb=tb, buf=buf:
                         e.matmul(pq[:], buf[:, chk * 128:(chk + 1) * 128],
                                  h_fm[:, chk * T + tb * 512:
                                       chk * T + tb * 512 + 512],
                                  start=(chk == 0), stop=(chk == 15)),
                         w2, eT)
                qmm = (eT, p.tot[eT])
                ev = p.do(eS, lambda e, pq=pq, tb=tb, qr=qr: e.copy(
                    qr[:, tb * 512:(tb + 1) * 512], pq[:]), [qmm], eS)
                rb_pq.mark(pseq, ev)
                ev_done.append(ev)
            rb_wq.mark(sl, (eT, p.tot[eT]))
            w4 = [ev_done[3], mcw] + rb_qc.use(sl)
            p.do(eV, lambda e, qr=qr, qcb=qcb, sl=sl:
                 e.tensor_scalar_mul(qcb[:], qr[:],
                                     cw_sb[:, sl * 4 + 3:sl * 4 + 4]),
                 w4, eV)
            for j in (2, 1, 0):
                sh = 3 - j
                p.do(eV, lambda e, qr=qr, qcb=qcb, sl=sl, j=j, sh=sh:
                     e.scalar_tensor_tensor(
                         qcb[:, sh:T], qr[:, 0:T - sh],
                         cw_sb[:, sl * 4 + j:sl * 4 + j + 1],
                         qcb[:, sh:T], ALU.mult, ALU.add), (), eV)
            conv_done = [(eV, p.tot[eV])]
            rb_qr.mark(sl, *conv_done)
            if sl < 16:
                dst = s_qk[0, sl, :, :]
            elif sl < 32:
                dst = s_qk[1, sl - 16, :, :]
            else:
                dst = s_v[sl - 32, :, :]
            st = p.dma("scalar", dst, qcb[:], f"qst{sl % 2}",
                       waits=conv_done)
            qc_marks = [st]
            if sl < 32:
                sq = sqb[sl % 2]
                sv = p.do(eS, lambda e, qcb=qcb, sq=sq: e.activation(
                    sq[:], qcb[:], AF.Square),
                    conv_done + rb_sq.use(sl), eS)
                qc_marks.append(sv)
                g32 = sl % 32
                for tb in range(4):
                    w6 = ([sv] + (rb_ss.use(0) if g32 == 0 else []))
                    p.do(eT, lambda e, tb=tb, sq=sq, g32=g32: e.matmul(
                        ps_ss[tb][:], eye32[:, g32 * 32:(g32 + 1) * 32],
                        sq[:, tb * 512:(tb + 1) * 512],
                        start=(g32 == 0), stop=(g32 == 31)), w6, eT)
                rb_sq.mark(sl, (eT, p.tot[eT]))
                if g32 == 31:
                    mm_end = (eT, p.tot[eT])
                    evs = []
                    for tb in range(4):
                        ev = p.do(eV, lambda e, tb=tb: e.tensor_copy(
                            ss_all[0:32, tb * 512:(tb + 1) * 512],
                            ps_ss[tb][:]), [mm_end], eV)
                        evs.append(ev)
                    rb_ss.mark(0, *evs)
            rb_qc.mark(sl, *qc_marks)
        m = p.do(eV, lambda e: e.tensor_scalar_max(
            ss_f[:], ss_all[:], 1e-24), [(eV, p.tot[eV])], eV)
        m = p.do(eS, lambda e: e.activation(ss_f[:], ss_f[:], AF.Ln),
                 [m], eS)
        m = p.do(eS, lambda e: e.activation(ss_f[:], ss_f[:], AF.Exp,
                                            scale=-0.5), (), eS)
        m = p.do(eV, lambda e: e.tensor_copy(rn_hi[:], ss_f[:]), [m], eV)
        m = p.do(eV, lambda e: e.tensor_sub(ss_all[:], ss_f[:], rn_hi[:]),
                 (), eV)
        m = p.do(eV, lambda e: e.tensor_copy(rn_lo[:], ss_all[:]), (), eV)
        if "rn" in dbg:
            p.dma("scalar", dbg_t["rn"], ss_f[:], "dbg", waits=[m])

    ph12.close()
    p.barrier()
    if n_phases <= 2:
        _finish(p, nc)
        return nc


    with contextlib.ExitStack() as ph:
        def psb(n_, s_, d_):
            return ph.enter_context(nc.sbuf_tensor(n_, s_, d_))

        def pps(n_, s_, d_=F32):
            return ph.enter_context(nc.psum_tensor(n_, s_, d_))

        qkbuf = [psb(f"qkbuf{i}", [128, 4096], BF16) for i in range(2)]
        vbuf = [psb("vbuf0", [128, 4096], BF16)] * 2
        zsb = [psb("zsb0", [128, ZN], BF16)] * 2
        qn = psb("qn", [128, 2048], BF16)
        kn = psb("kn", [128, 2048], BF16)
        gns = psb("gns", [128, 2048], BF16)
        gnt = psb("gnt", [128, 2048], BF16)
        gq = psb("gq", [128, 2048], BF16)
        ktm = psb("ktm", [128, 2048], BF16)
        grow3 = psb("grow3", [32, 384], F32)
        ghb = psb("ghb", [32, 128], F32)
        gtmp = psb("gtmp", [32, 384], F32)
        ghi = psb("ghi", [32, 384], BF16)
        glo = psb("glo", [32, 384], BF16)
        wrow = psb("wrow", [32, 128], F32)
        cols_tm = psb("cols_tm", [128, 128], F32)
        ecols = psb("ecols", [128, 128], F32)
        exbuf = [psb(f"exbuf{i}", [128, 384], BF16) for i in range(2)]
        exin = psb("exin", [128, 384], F32)
        X4all = psb("X4all", [128, 4096], BF16)
        XT4all = psb("XT4all", [128, 4096], BF16)
        Rall = psb("Rall", [128, 4096], BF16)
        Xp = [psb(f"Xp{i}", [128, 512], BF16) for i in range(2)]
        XTp = [psb(f"XTp{i}", [128, 512], BF16) for i in range(2)]
        wotall = psb("wotall", [128, 4096], BF16)
        O4all = psb("O4all", [128, 4096], BF16)
        O4b = psb("O4b", [128, 4096], BF16)
        osq = psb("osq", [128, 512], F32)
        ssqall = psb("ssqall", [128, 32], F32)
        rmsall = psb("rmsall", [128, 32], F32)
        btmp = psb("btmp", [128, 128], BF16)
        rsc = psb("rsc", [128, 128], BF16)
        Mall = [psb("Mall0", [128, 4096], F32)] * 2
        Mtmp = psb("Mtmp", [128, 128], F32)
        Mball = psb("Mball", [128, 4096], BF16)
        
        gatedb = [psb("gatedb0", [128, ZN], BF16)] * 2

        psA = pps("psA", [128, 512])
        psB = pps("psB", [128, 512])
        psT = pps("psT", [128, 1024], BF16)
        psC = pps("psC", [128, 512])
        psD = pps("psD", [128, 512])
        psR = pps("psR", [128, 512])
        psSQ = pps("psSQ", [128, 512])
        psSQT = pps("psSQT", [128, 512])

        m0 = p.do(eV, lambda e: e.memset(Mall[0][:], 0.0), (), eV)
        m0b = p.do(eV, lambda e: e.memset(Mball[:], 0.0), (), eV)

        qk4 = s_qk.rearrange("a k p t -> p a k t")
        v4 = s_v.rearrange("v p t -> p v t")

        rb_qkb = Rot(2)
        rb_exin = Rot(1)
        rb_vb = Rot(1)
        rb_zsb = Rot(1)
        rb_ex = Rot(2)
        rb_cd = Rot(2)
        rb_gt = Rot(2)
        rb_psT = Rot(8)
        rb_gb = Rot(1)
        cb = [m0, m0b]
        for ci in range(P3_CHUNKS):
            mqk = p.dma("sync", qkbuf[ci % 2][:],
                        qk4[:, :, :, ci * 128:(ci + 1) * 128],
                        f"qk{ci % 2}", waits=rb_qkb.use(ci))
            mv = p.dma("sync", vbuf[ci % 2][:],
                       v4[:, :, ci * 128:(ci + 1) * 128],
                       f"v{ci % 2}", waits=rb_vb.use(ci))
            mz = p.dma("sync", zsb[ci % 2][:],
                       s_zs[ci * 128:(ci + 1) * 128, :],
                       f"zs{ci % 2}", waits=rb_zsb.use(ci))
            ld_sl = ldrT[:, ci * 128:(ci + 1) * 128]
            lbn_sl = lbnrT[:, ci * 128:(ci + 1) * 128]
            m = p.do(eV, lambda e, ld_sl=ld_sl: e.tensor_tensor_scan(
                grow3[:, 256:384], ones32[0:32, :], ld_sl, 0.0,
                ALU.mult, ALU.add), cb, eV)
            mg = p.do(eS, lambda e: e.activation(
                grow3[:, 0:128], grow3[:, 256:384], AF.Identity,
                scale=cst[0:32, C_NEGONE:C_NEGONE + 1]), [m] + cb, eS)
            m2 = p.do(eV, lambda e, ld_sl=ld_sl: e.scalar_tensor_tensor(
                ghb[:], ld_sl, -1.0, grow3[:, 256:384], ALU.mult, ALU.add),
                [m], eV)
            m2 = p.do(eV, lambda e, lbn_sl=lbn_sl: e.tensor_sub(
                grow3[:, 128:256], ghb[:], lbn_sl), (), eV)
            m3 = p.do(eV, lambda e: e.tensor_copy(ghi[:], grow3[:]),
                      [m2, mg], eV)
            m3 = p.do(eV, lambda e: e.tensor_sub(gtmp[:], grow3[:], ghi[:]),
                      (), eV)
            m3 = p.do(eV, lambda e: e.tensor_copy(glo[:], gtmp[:]), (), eV)
            mw = p.do(eS, lambda e: e.activation(
                wrow[:], grow3[:, 256:384], AF.Exp,
                scale=cst[0:32, C_NEGONE:C_NEGONE + 1],
                bias=grow3[:, 383:384]), [m] + cb, eS)
            t1 = p.do(eT, lambda e: e.transpose(
                psA[:, 0:32], grow3[:, 256:384], identf[0:32, 0:32]),
                [m] + cb, eT)
            t1 = p.do(eT, lambda e: e.transpose(
                psA[:, 32:64], grow3[:, 128:256], identf[0:32, 0:32]),
                [m2], eT)
            t1 = p.do(eT, lambda e: e.transpose(
                psA[:, 64:96], wrow[:], identf[0:32, 0:32]), [mw], eT)
            cv = p.do(eV, lambda e: e.tensor_copy(cols_tm[:, 0:96],
                                                  psA[:, 0:96]), [t1], eV)
            s1 = p.do(eS, lambda e: e.activation(
                cols_tm[:, 96:128], cols_tm[:, 0:32], AF.Identity,
                scale=cst[:, C_NEGONE:C_NEGONE + 1]), [cv], eS)
            s1 = p.do(eS, lambda e: e.activation(
                ecols[:, 0:64], cols_tm[:, 0:64], AF.Exp), (), eS)
            s1 = p.do(eS, lambda e: e.activation(
                ecols[:, 64:96], ecols[:, 32:64], AF.Identity,
                scale=cst[:, C_NEGONE:C_NEGONE + 1]), (), eS)
            egv = p.do(eV, lambda e: e.tensor_mul(
                ecols[:, 96:128], cols_tm[:, 64:96], ecols[:, 0:32]),
                [s1], eV)
            gn_done = []
            for kh in range(NKH):
                bank = psA if kh % 2 == 0 else psB
                w_ = rb_gt.use(kh) + [mqk] + cb
                if kh == 0:
                    w_ = w_ + [cv]
                p.do(eT, lambda e, bank=bank, kh=kh, ci=ci: e.matmul(
                    bank[:, 0:128], sel32[:, kh * 128:(kh + 1) * 128],
                    rn_hi[0:32, ci * 128:(ci + 1) * 128],
                    start=True, stop=False), w_, eT)
                t2 = p.do(eT, lambda e, bank=bank, kh=kh, ci=ci: e.matmul(
                    bank[:, 0:128], sel32[:, kh * 128:(kh + 1) * 128],
                    rn_lo[0:32, ci * 128:(ci + 1) * 128],
                    start=False, stop=True), (), eT)
                vq = p.do(eV, lambda e, bank=bank, kh=kh, ci=ci: e.tensor_mul(
                    qn[:, kh * 128:(kh + 1) * 128], bank[:, 0:128],
                    qkbuf[ci % 2][:, kh * 128:(kh + 1) * 128]), [t2], eV)
                p.do(eT, lambda e, bank=bank, kh=kh, ci=ci: e.matmul(
                    bank[:, 128:256],
                    sel32[:, (16 + kh) * 128:(17 + kh) * 128],
                    rn_hi[0:32, ci * 128:(ci + 1) * 128],
                    start=True, stop=False), (), eT)
                t3 = p.do(eT, lambda e, bank=bank, kh=kh, ci=ci: e.matmul(
                    bank[:, 128:256],
                    sel32[:, (16 + kh) * 128:(17 + kh) * 128],
                    rn_lo[0:32, ci * 128:(ci + 1) * 128],
                    start=False, stop=True), (), eT)
                vk = p.do(eV, lambda e, bank=bank, kh=kh, ci=ci: e.tensor_mul(
                    kn[:, kh * 128:(kh + 1) * 128], bank[:, 128:256],
                    qkbuf[ci % 2][:, 2048 + kh * 128:2048 + (kh + 1) * 128]),
                    [t3], eV)
                t4 = p.do(eT, lambda e, bank=bank, kh=kh: e.matmul(
                    bank[:, 256:384], kn[:, kh * 128:(kh + 1) * 128],
                    kn[:, kh * 128:(kh + 1) * 128], start=True, stop=True),
                    [vk], eT)
                t5 = p.do(eT, lambda e, bank=bank, kh=kh: e.matmul(
                    bank[:, 384:512], kn[:, kh * 128:(kh + 1) * 128],
                    qn[:, kh * 128:(kh + 1) * 128], start=True, stop=True),
                    [vq], eT)
                g1m = p.do(eV, lambda e, bank=bank, kh=kh: e.tensor_mul(
                    gns[:, kh * 128:(kh + 1) * 128], bank[:, 256:384],
                    gmask[:, 0:128]), [t4], eV)
                g2m = p.do(eV, lambda e, bank=bank, kh=kh: e.tensor_mul(
                    gnt[:, kh * 128:(kh + 1) * 128], bank[:, 256:384],
                    gmask[:, 128:256]), (), eV)
                g3m = p.do(eV, lambda e, bank=bank, kh=kh: e.tensor_mul(
                    gq[:, kh * 128:(kh + 1) * 128], bank[:, 384:512],
                    gmask[:, 256:384]), [t5], eV)
                slot = kh % 8
                t6 = p.do(eT, lambda e, kh=kh, slot=slot: e.transpose(
                    psT[:, slot * 128:(slot + 1) * 128],
                    kn[:, kh * 128:(kh + 1) * 128], identb[:]),
                    [vk] + rb_psT.use(slot), eT)
                sc1 = p.do(eS, lambda e, kh=kh, slot=slot: e.copy(
                    ktm[:, kh * 128:(kh + 1) * 128],
                    psT[:, slot * 128:(slot + 1) * 128]), [t6], eS)
                rb_psT.mark(slot, sc1)
                rb_gt.mark(kh, g1m, g2m, g3m)
                gn_done.append((g1m, g2m, g3m))
            bB = []
            for vh in range(NVH):
                kh = vh // 2
                pc = psC if vh % 2 == 0 else psD
                w_ = rb_cd.use(vh) + [m3]
                p.do(eT, lambda e, pc=pc, vh=vh: e.matmul(
                    pc[:, 0:384], sel32[:, vh * 128:(vh + 1) * 128],
                    ghi[:], start=True, stop=False), w_, eT)
                t7 = p.do(eT, lambda e, pc=pc, vh=vh: e.matmul(
                    pc[:, 0:384], sel32[:, vh * 128:(vh + 1) * 128],
                    glo[:], start=False, stop=True), (), eT)
                ex = exbuf[vh % 2]
                w2 = rb_ex.use(vh) + [t7, cv, s1]
                c1_ = p.do(eV, lambda e, pc=pc, vh=vh: e.tensor_scalar(
                    exin[:, 0:128], pc[:, 0:128],
                    cols_tm[:, 32 + vh:33 + vh], 0.0, ALU.add, ALU.min),
                    w2 + rb_exin.use(vh), eV)
                c2_ = p.do(eV, lambda e, pc=pc, vh=vh: e.tensor_scalar(
                    exin[:, 128:384], pc[:, 128:384],
                    cols_tm[:, 96 + vh:97 + vh], 0.0, ALU.add, ALU.min),
                    (), eV)
                rb_cd.mark(vh, c2_)
                sx = p.do(eS, lambda e, ex=ex: e.activation(
                    ex[:], exin[:], AF.Exp), [c2_], eS)
                rb_exin.mark(vh, sx)
                kk = gn_done[kh]
                x1 = p.do(eV, lambda e, ex=ex, vh=vh, kh=kh: e.tensor_mul(
                    X4all[:, vh * 128:(vh + 1) * 128], ex[:, 0:128],
                    gns[:, kh * 128:(kh + 1) * 128]), [sx, kk[0]], eV)
                x2 = p.do(eV, lambda e, ex=ex, vh=vh, kh=kh: e.tensor_mul(
                    XT4all[:, vh * 128:(vh + 1) * 128], ex[:, 128:256],
                    gnt[:, kh * 128:(kh + 1) * 128]), [kk[1]], eV)
                x3 = p.do(eV, lambda e, ex=ex, vh=vh, kh=kh: e.tensor_mul(
                    wotall[:, vh * 128:(vh + 1) * 128], ex[:, 256:384],
                    gq[:, kh * 128:(kh + 1) * 128]), [kk[2]], eV)
                rb_ex.mark(vh, x3)
                bank = psA if vh % 2 == 0 else psB
                w3 = rb_gt.use(16 + vh)
                zk = p.do(eT, lambda e, bank=bank, vh=vh, kh=kh: e.matmul(
                    bank[:, 0:128], kn[:, kh * 128:(kh + 1) * 128],
                    Mball[:, vh * 128:(vh + 1) * 128],
                    start=True, stop=True), w3 + cb, eT)
                zq = p.do(eT, lambda e, bank=bank, vh=vh, kh=kh: e.matmul(
                    bank[:, 128:256], qn[:, kh * 128:(kh + 1) * 128],
                    Mball[:, vh * 128:(vh + 1) * 128],
                    start=True, stop=True), (), eT)
                ev1 = p.do(eV, lambda e, bank=bank, vh=vh: e.tensor_scalar(
                    O4all[:, vh * 128:(vh + 1) * 128], bank[:, 128:256],
                    ecols[:, 0 + vh:1 + vh] if False
                    else ecols[:, vh:vh + 1], None,
                    ALU.mult), [zq, egv], eV)
                slot = vh % 8
                t8 = p.do(eT, lambda e, vh=vh, slot=slot, ci=ci: e.transpose(
                    psT[:, slot * 128:(slot + 1) * 128],
                    vbuf[ci % 2][:, vh * 128:(vh + 1) * 128], identb[:]),
                    [mv] + rb_psT.use(slot), eT)
                bv1 = p.do(eV, lambda e, vh=vh, slot=slot, ci=ci:
                           e.tensor_scalar(
                               btmp[:], psT[:, slot * 128:(slot + 1) * 128],
                               beta_all[:, ci * 32 + vh:ci * 32 + vh + 1],
                               None, ALU.mult), [t8], eV)
                rb_psT.mark(slot, bv1)
                bv2 = p.do(eV, lambda e, bank=bank, vh=vh:
                           e.scalar_tensor_tensor(
                               Rall[:, vh * 128:(vh + 1) * 128],
                               bank[:, 0:128],
                               ecols[:, 64 + vh:65 + vh], btmp[:],
                               ALU.mult, ALU.add), [zk, bv1], eV)
                rb_gt.mark(16 + vh, ev1, bv2)
                bB.append(bv2)
            sol_done = []
            for g in range(8):
                gc = slice(g * 512, (g + 1) * 512)
                xev = yev = None
                for lvl in range(7):
                    if lvl == 0:
                        Xs = lambda j, g=g: X4all[:, g * 512 + j * 128:
                                                  g * 512 + (j + 1) * 128]
                        XTs = lambda j, g=g: XT4all[:, g * 512 + j * 128:
                                                    g * 512 + (j + 1) * 128]
                    else:
                        Xs = lambda j, b=(lvl % 2): Xp[b][:, j * 128:
                                                         (j + 1) * 128]
                        XTs = lambda j, b=(lvl % 2): XTp[b][:, j * 128:
                                                            (j + 1) * 128]
                    w_ = []
                    if lvl == 0:
                        w_ = [bB[g * 4 + j] for j in range(4)]
                    for j in range(4):
                        vh = g * 4 + j
                        ww = list(w_) if j == 0 else []
                        if j == 0:
                            if lvl == 0:
                                ww += [x1, x2, x3]
                            ww += sol_done[-1] if sol_done else []
                        p.do(eT, lambda e, j=j, Xs=Xs, XTs=XTs, vh=vh:
                             e.matmul(psR[:, j * 128:(j + 1) * 128], XTs(j),
                                      Rall[:, vh * 128:(vh + 1) * 128],
                                      start=True, stop=False), ww, eT)
                        p.do(eT, lambda e, j=j, vh=vh: e.matmul(
                            psR[:, j * 128:(j + 1) * 128], identb[:],
                            Rall[:, vh * 128:(vh + 1) * 128],
                            start=False, stop=True), (), eT)
                        if lvl < 6:
                            p.do(eT, lambda e, j=j, Xs=Xs, XTs=XTs: e.matmul(
                                psSQ[:, j * 128:(j + 1) * 128], XTs(j),
                                Xs(j), start=True, stop=True), (), eT)
                            p.do(eT, lambda e, j=j, Xs=Xs, XTs=XTs: e.matmul(
                                psSQT[:, j * 128:(j + 1) * 128], Xs(j),
                                XTs(j), start=True, stop=True), (), eT)
                    tmark = (eT, p.tot[eT])
                    rv1 = p.do(eV, lambda e, gc=gc: e.tensor_copy(
                        Rall[:, gc], psR[:]), [tmark], eV)
                    evs = [rv1]
                    if lvl < 6:
                        nb = (lvl + 1) % 2
                        ev2 = p.do(eS, lambda e, nb=nb: e.copy(
                            Xp[nb][:], psSQ[:]), [tmark], eS)
                        ev3 = p.do(eV, lambda e, nb=nb: e.tensor_copy(
                            XTp[nb][:], psSQT[:]), [tmark], eV)
                        evs += [ev2, ev3]
                    sol_done.append(evs)
            sol_last = sol_done[-1]
            rb_rsc = Rot(1)
            rb_osq = Rot(1)
            st_done = []
            for vh in range(NVH):
                kh = vh // 2
                bank = psA if vh % 2 == 0 else psB
                w_ = rb_gt.use(48 + vh) + sol_done[(vh // 4) * 7 + 6] + [x3]
                om = p.do(eT, lambda e, bank=bank, vh=vh: e.matmul(
                    bank[:, 0:128], wotall[:, vh * 128:(vh + 1) * 128],
                    Rall[:, vh * 128:(vh + 1) * 128],
                    start=True, stop=True), w_, eT)
                ov = p.do(eV, lambda e, bank=bank, vh=vh: e.tensor_add(
                    O4b[:, vh * 128:(vh + 1) * 128], bank[:, 0:128],
                    O4all[:, vh * 128:(vh + 1) * 128]), [om, ev1], eV)
                rs = p.do(eV, lambda e, vh=vh: e.tensor_scalar(
                    rsc[:], Rall[:, vh * 128:(vh + 1) * 128],
                    cols_tm[:, 64 + vh:65 + vh], None, ALU.mult),
                    [cv] + sol_done[(vh // 4) * 7 + 6] + rb_rsc.use(0), eV)
                sm = p.do(eT, lambda e, bank=bank, vh=vh, kh=kh: e.matmul(
                    bank[:, 128:256], ktm[:, kh * 128:(kh + 1) * 128],
                    rsc[:], start=True, stop=True), [rs, sc1], eT)
                rb_rsc.mark(0, sm)
                mu_ = p.do(eV, lambda e, vh=vh: e.tensor_scalar(
                    Mtmp[:], Mall[0][:, vh * 128:(vh + 1) * 128],
                    ecols[:, 96 + vh:97 + vh], None, ALU.mult),
                    [sm, egv], eV)
                mu_ = p.do(eV, lambda e, bank=bank, vh=vh: e.tensor_add(
                    Mall[0][:, vh * 128:(vh + 1) * 128], Mtmp[:],
                    bank[:, 128:256]), (), eV)
                mb_ = p.do(eS, lambda e, vh=vh: e.copy(
                    Mball[:, vh * 128:(vh + 1) * 128],
                    Mall[0][:, vh * 128:(vh + 1) * 128]), [mu_], eS)
                rb_gt.mark(48 + vh, ov, mb_)
                st_done.append((ov, mu_, mb_))
                if vh % 4 == 3:
                    gg = vh // 4
                    sq_ = p.do(eS, lambda e, gg=gg: e.activation(
                        osq[:], O4b[:, gg * 512:(gg + 1) * 512], AF.Square),
                        [ov] + [st_done[gg * 4 + j][0] for j in range(4)]
                        + rb_osq.use(0), eS)
                    rd = p.do(eV, lambda e, gg=gg: e.tensor_reduce(
                        ssqall[:, gg * 4:(gg + 1) * 4],
                        osq.ap().rearrange("p (v d) -> p v d", d=128),
                        AX.X, ALU.add), [sq_], eV)
                    rb_osq.mark(0, rd)
            rdall = (eV, p.tot[eV])
            sr = p.do(eS, lambda e: e.activation(
                rmsall[:], ssqall[:], AF.Ln,
                scale=cst[:, C_INV_DV:C_INV_DV + 1],
                bias=cst[:, C_EPS_RMS:C_EPS_RMS + 1]), [rdall], eS)
            sr = p.do(eS, lambda e: e.activation(
                rmsall[:], rmsall[:], AF.Exp,
                scale=cst[:, C_NEGHALF:C_NEGHALF + 1]), (), eS)
            gb_ = gatedb[ci % 2]
            gv_last = None
            for vh in range(NVH):
                p.do(eV, lambda e, vh=vh: e.scalar_tensor_tensor(
                    btmp[:], O4b[:, vh * 128:(vh + 1) * 128],
                    rmsall[:, vh:vh + 1], nwrep[:, vh * 128:(vh + 1) * 128],
                    ALU.mult, ALU.mult),
                    [sr, mz] + (rb_gb.use(ci) if vh == 0 else []), eV)
                gv_last = p.do(eV, lambda e, vh=vh, gb_=gb_, ci=ci:
                               e.tensor_mul(
                                   gb_[:, vh * 128:(vh + 1) * 128], btmp[:],
                                   zsb[ci % 2][:, vh * 128:(vh + 1) * 128]),
                               (), eV)
            rb_zsb.mark(ci, gv_last)
            mst = p.dma("scalar", s_gated[ci * 128:(ci + 1) * 128, :],
                        gb_[:], f"gst{ci % 2}", waits=[gv_last])
            rb_gb.mark(ci, mst)
            cb = [(eT, p.tot[eT]), (eV, p.tot[eV]), (eS, p.tot[eS])]
            rb_qkb.mark(ci, *cb)
            rb_vb.mark(ci, *cb)
        if "gated" in dbg:
            p.dma("gpsimd", dbg_t["gated"], s_gated[:, :], "dbg",
                  waits=cb + [("gst0", p.tot["gst0"]),
                              ("gst1", p.tot["gst1"])])
            p.dma("gpsimd", dbg_t["mfin"], Mball[:], "dbg", waits=cb)

    p.barrier()
    if n_phases <= 3:
        _finish(p, nc)
        return nc


    with contextlib.ExitStack() as ph:
        def psb(n_, s_, d_):
            return ph.enter_context(nc.sbuf_tensor(n_, s_, d_))

        def pps(n_, s_, d_=F32):
            return ph.enter_context(nc.psum_tensor(n_, s_, d_))

        gat = psb("gat", [128, 32 * 512], BF16)
        xq = [psb(f"xq{i}", [128, 512], F32) for i in range(2)]
        x2g = psb("x2g", [128, 4 * D], F32)
        h2b = [psb(f"h2b{i}", [128, D], BF16) for i in range(2)]
        y1t = psb("y1t", [128, 512], F32)
        gat_t = psb("gat_t", [128, D], BF16)
        cb1_t = psb("cb1_t", [128, D], BF16)
        sb_t = psb("sb_t", [128, D], BF16)
        mseg = psb("mseg3", [1, D], BF16)
        mseg2 = psb("mseg3b", [1, D], BF16)
        stat3 = [psb(f"stat3{i}", [128, 6], F32) for i in range(2)]
        sqd3 = psb("sqd3", [128, D], F32)
        sqd3b = psb("sqd3b", [128, D], BF16)
        ps_y = [pps(f"ps_y{i}", [128, 512]) for i in range(8)]
        ps_m3 = ps_y[0]

        prev = None
        for seg, dst in ((2, gat_t), (4, cb1_t), (3, sb_t)):
            mld = p.dma("sync", mseg[:], s_mod[0:1, seg * D:(seg + 1) * D],
                        "m3h", waits=([prev] if prev else []))
            mld2 = p.dma("sync", mseg2[:], s_mod[1:2, seg * D:(seg + 1) * D],
                         "m3l", waits=([prev] if prev else []))
            for blk in range(4):
                off = blk * 512
                w0 = ([prev] if prev else []) + [mld, mld2]
                p.do(eT, lambda e, off=off, mseg=mseg: e.matmul(
                    ps_m3[:], ones1b[0:1, :], mseg[0:1, off:off + 512],
                    start=True, stop=False), w0, eT)
                tm = p.do(eT, lambda e, off=off, mseg2=mseg2: e.matmul(
                    ps_m3[:], ones1b[0:1, :], mseg2[0:1, off:off + 512],
                    start=False, stop=True), (), eT)
                prev = p.do(eV, lambda e, d=dst, blk=blk: e.tensor_copy(
                    d[:, blk * 512:(blk + 1) * 512], ps_m3[:]), [tm], eV)
        vmod3 = prev
        if "m3" in dbg:
            p.dma("gpsimd", dbg_t["m3"][:, 0:D], gat_t[:], "dbg",
                  waits=[vmod3])
            p.dma("gpsimd", dbg_t["m3"][:, D:2 * D], cb1_t[:], "dbg",
                  waits=[vmod3])
            p.dma("gpsimd", dbg_t["m3"][:, 2 * D:3 * D], sb_t[:], "dbg",
                  waits=[vmod3])
            p.dma("gpsimd", dbg_t["smod"][:, :], s_mod[:, :], "dbg",
                  waits=[vmod3])

        xld = {}
        rb_wo = Rot(1)
        rb_x3 = Rot(2)
        rb_y1 = Rot(1)
        rb_x2g = Rot(4)
        rb_h2b = Rot(2)
        rb_st3 = Rot(2)
        rb_psy = Rot(2)
        rb_gat = Rot(1)
        woq = psb("woq", [128, 32 * 512], BF16)
        for g2 in range(4):
            t0 = g2 * 512
            glds = []
            for dch in range(32):
                gld = p.dmat("sync", gat[:, dch * 512:(dch + 1) * 512],
                             s_gated[t0:t0 + 512, dch * 128:(dch + 1) * 128],
                             f"gatld{dch % 2}",
                             waits=(rb_gat.use(0) if dch < 2 else []) +
                             ([("gst0", p.tot["gst0"]),
                               ("gst1", p.tot["gst1"])] if g2 == 0 and
                              dch < 2 else []))
                glds = glds[-1:] + [gld]
            gld = None
            for cq in range(4):
                c0 = cq * 512
                wlds = []
                for dch in range(32):
                    wld = p.dma("gpsimd", woq[:, dch * 512:(dch + 1) * 512],
                                woutT[dch * 128:(dch + 1) * 128,
                                      c0:c0 + 512],
                                f"woq{dch % 2}",
                                waits=(rb_wo.use(0) if dch < 2 else []))
                    wlds = wlds[-1:] + [wld]
                wld = None
                for tt_ in range(4):
                    ti = g2 * 4 + tt_
                    pyb = ps_y[(cq * 4 + tt_) % 2]
                    mm_w = wlds + glds + [vmod3] + rb_psy.use(cq * 4 + tt_)
                    for dch in range(32):
                        p.do(eT, lambda e, pyb=pyb, dch=dch, tt_=tt_:
                             e.matmul(
                                 pyb[:],
                                 gat[:, dch * 512 + tt_ * 128:
                                     dch * 512 + tt_ * 128 + 128],
                                 woq[:, dch * 512:(dch + 1) * 512],
                                 start=(dch == 0), stop=(dch == 31)),
                             mm_w if dch == 0 else (), eT)
                    ymm = (eT, p.tot[eT])
                    xb = xq[tt_ % 2]
                    mx = p.dma("sync", xb[:],
                               x_in[ti * 128:(ti + 1) * 128, c0:c0 + 512],
                               f"x3{tt_ % 2}", waits=rb_x3.use(tt_))
                    x2o = x2g[:, tt_ * D:(tt_ + 1) * D]
                    m1_ = p.do(eV, lambda e, pyb=pyb, c0=c0: e.tensor_mul(
                        y1t[:], pyb[:], gat_t[:, c0:c0 + 512]),
                        [ymm, vmod3] + rb_y1.use(0), eV)
                    m2_ = p.do(eV, lambda e, xb=xb, x2o=x2o, c0=c0:
                               e.tensor_add(x2o[:, c0:c0 + 512], y1t[:],
                                            xb[:]),
                               [mx] + (rb_x2g.use(tt_) if cq == 0 else []),
                               eV)
                    rb_y1.mark(0, m2_)
                    rb_x3.mark(tt_, m2_)
                    rb_psy.mark(cq * 4 + tt_, m1_)
                    if cq == 3:
                        st3 = stat3[tt_ % 2]
                        sx2 = p.dma("scalar",
                                    s_x2[ti * 128:(ti + 1) * 128, :],
                                    x2o[:, :], f"x2st{tt_ % 2}",
                                    waits=[m2_])
                        mv_ = p.do(eV, lambda e, x2o=x2o, s_=st3:
                                   e.reduce_sum(s_[:, 0:1], x2o[:, :], AX.X),
                                   [m2_] + rb_st3.use(tt_), eV)
                        ms_ = p.do(eS, lambda e, s_=st3: e.activation(
                            s_[:, 4:5], s_[:, 0:1], AF.Identity,
                            scale=cst[:, C_NEGINV_D:C_NEGINV_D + 1]),
                            [mv_], eS)
                        ms_ = p.do(eS, lambda e, x2o=x2o, s_=st3:
                                   e.activation(
                                       sqd3[:], x2o[:], AF.Square,
                                       bias=s_[:, 4:5],
                                       accum_out=s_[:, 1:2]), (), eS)
                        ms_ = p.do(eS, lambda e, s_=st3: e.activation(
                            s_[:, 2:3], s_[:, 1:2], AF.Ln,
                            scale=cst[:, C_INV_D:C_INV_D + 1],
                            bias=cst[:, C_EPS_LN:C_EPS_LN + 1]), (), eS)
                        ms_ = p.do(eS, lambda e, s_=st3: e.activation(
                            s_[:, 3:4], s_[:, 2:3], AF.Exp,
                            scale=cst[:, C_NEGHALF:C_NEGHALF + 1]), (), eS)
                        hv_ = p.do(eV, lambda e, x2o=x2o, s_=st3:
                                   e.tensor_scalar(
                                       sqd3[:], x2o[:, :], s_[:, 4:5],
                                       s_[:, 3:4], ALU.add, ALU.mult),
                                   [ms_], eV)
                        rb_st3.mark(tt_, hv_)
                        hv_ = p.do(eV, lambda e: e.tensor_mul(
                            sqd3b[:], sqd3[:], cb1_t[:]), (), eV)
                        hb_ = h2b[tt_ % 2]
                        hv_ = p.do(eV, lambda e, hb_=hb_: e.tensor_add(
                            hb_[:], sqd3b[:], sb_t[:]),
                            rb_h2b.use(tt_), eV)
                        sh2 = p.dma("scalar",
                                    s_h2[ti * 128:(ti + 1) * 128, :],
                                    hb_[:], f"h2st{tt_ % 2}", waits=[hv_])
                        rb_h2b.mark(tt_, sh2)
                        rb_x2g.mark(tt_, sx2, hv_)
                rb_wo.mark(0, (eT, p.tot[eT]))
            rb_gat.mark(0, (eT, p.tot[eT]))
        if "m3e" in dbg:
            w_ = [(s, p.tot[s]) for s in ("x2st0", "x2st1")]
            w_ += [(eV, p.tot[eV]), (eS, p.tot[eS]), (eT, p.tot[eT])]
            p.dma("gpsimd", dbg_t["m3e"][:, 0:D], gat_t[:], "dbg", waits=w_)
            p.dma("gpsimd", dbg_t["m3e"][:, D:2 * D], cb1_t[:], "dbg",
                  waits=w_)
            p.dma("gpsimd", dbg_t["m3e"][:, 2 * D:3 * D], sb_t[:], "dbg",
                  waits=w_)
            p.dma("gpsimd", dbg_t["x2e"][:, :], s_x2[:, :], "dbg", waits=w_)

    p.barrier()
    if n_phases <= 4:
        for tok in dbg.split(",") if isinstance(dbg, str) else dbg:
            if tok.startswith("padT"):
                for _ in range(int(tok[4:] or "6000")):
                    p.do(eT, lambda e: e.nop(), (), None)
            if tok.startswith("sems"):
                for i in range(int(tok[4:] or "14")):
                    mk = p.do(eT, lambda e: e.nop(), (), p._sem(f"zz{i}"), 1)
                    p.do(eV, lambda e: e.nop(), [mk], None)
        _finish(p, nc)
        return nc

    with contextlib.ExitStack() as ph:
        def psb(n_, s_, d_):
            return ph.enter_context(nc.sbuf_tensor(n_, s_, d_))

        def pps(n_, s_, d_=F32):
            return ph.enter_context(nc.psum_tensor(n_, s_, d_))

        h2fm = psb("h2fm", [128, 16 * T], BF16)
        wg_sl = [psb(f"wg_sl{i}", [128, 16 * 256], BF16) for i in range(2)]
        wu_sl = [psb(f"wu_sl{i}", [128, 16 * 256], BF16) for i in range(2)]
        sg = psb("sg", [128, 256], F32)
        h1slab = psb("h1slab", [128, 16 * 256], BF16)
        sqslab = psb("sqslab", [128, 16 * 256], BF16)
        thr = psb("thr", [128, 1024], BF16)
        tmpa = psb("tmpa", [128, 1024], BF16)
        tmpb = psb("tmpb", [128, 1024], BF16)
        tmpc = psb("tmpc", [128, 1024], BF16)
        keep = psb("keep", [128, 1024], BF16)
        h1s_slab = psb("h1s_slab", [128, 16 * 256], BF16)
        ps_g = [pps(f"ps4g{i}", [128, 512]) for i in range(2)]
        ps_u = [pps(f"ps4u{i}", [128, 512]) for i in range(2)]

        hlds = []
        for cch in range(16):
            hld = p.dmat("sync", h2fm[:, cch * T:cch * T + T],
                         s_h2[0:T, cch * 128:(cch + 1) * 128],
                         f"h2ld{cch % 2}",
                         waits=[("h2st0", p.tot["h2st0"]),
                                ("h2st1", p.tot["h2st1"])]
                         if cch < 2 else [])
            hlds = hlds[-1:] + [hld]
        hld = None
        wg4 = wgate.rearrange("(ch p) n -> p ch n", p=128)
        wu4 = wup.rearrange("(ch p) n -> p ch n", p=128)
        rb_wg = Rot(2)
        rb_wu = Rot(2)
        rb_pg = Rot(2)
        rb_pu = Rot(2)
        rb_slab = Rot(1)
        rb_h1s = Rot(1)
        for ns in range(32):
            wgl = p.dma("gpsimd", wg_sl[ns % 2][:],
                        wg4[:, :, ns * 256:(ns + 1) * 256], f"wg{ns % 2}",
                        waits=rb_wg.use(ns))
            wul = p.dma("gpsimd", wu_sl[ns % 2][:],
                        wu4[:, :, ns * 256:(ns + 1) * 256], f"wu{ns % 2}",
                        waits=rb_wu.use(ns))
            slab_marks = []
            for tt_ in range(16):
                pg_ = ps_g[tt_ % 2]
                pu_ = ps_u[tt_ % 2]
                w_ = [wgl, wul] + hlds + rb_pg.use(tt_) + rb_pu.use(tt_) \
                    + (rb_slab.use(0) if tt_ == 0 else [])
                for cch in range(16):
                    p.do(eT, lambda e, pg_=pg_, cch=cch, tt_=tt_, ns=ns:
                         e.matmul(pg_[:, 0:256],
                                  h2fm[:, cch * T + tt_ * 128:
                                       cch * T + tt_ * 128 + 128],
                                  wg_sl[ns % 2][:, cch * 256:
                                                (cch + 1) * 256],
                                  start=(cch == 0), stop=(cch == 15)),
                         w_ if cch == 0 else (), eT)
                for cch in range(16):
                    p.do(eT, lambda e, pu_=pu_, cch=cch, tt_=tt_, ns=ns:
                         e.matmul(pu_[:, 0:256],
                                  h2fm[:, cch * T + tt_ * 128:
                                       cch * T + tt_ * 128 + 128],
                                  wu_sl[ns % 2][:, cch * 256:
                                                (cch + 1) * 256],
                                  start=(cch == 0), stop=(cch == 15)),
                         (), eT)
                gmm = (eT, p.tot[eT])
                sv_ = p.do(eS, lambda e, pg_=pg_: e.activation(
                    sg[:], pg_[:, 0:256], AF.Silu), [gmm], eS)
                hv_ = p.do(eV, lambda e, pu_=pu_, tt_=tt_: e.tensor_mul(
                    h1slab[:, tt_ * 256:(tt_ + 1) * 256], sg[:],
                    pu_[:, 0:256]), [sv_], eV)
                rb_pg.mark(tt_, sv_)
                rb_pu.mark(tt_, hv_)
                slab_marks.append(hv_)
            rb_wg.mark(ns, (eT, p.tot[eT]))
            rb_wu.mark(ns, (eT, p.tot[eT]))
            sq_ = p.do(eV, lambda e: e.tensor_mul(
                sqslab[:], h1slab[:], h1slab[:]), slab_marks, eV)
            sqv = sqslab.ap().rearrange("p (n four) -> p n four", four=4)
            a0, a1 = sqv[:, :, 0], sqv[:, :, 1]
            a2, a3 = sqv[:, :, 2], sqv[:, :, 3]
            m_ = p.do(eV, lambda e: e.tensor_max(tmpa[:], a0, a1), (), eV)
            m_ = p.do(eV, lambda e: e.tensor_tensor(
                tmpb[:], a0, a1, ALU.min), (), eV)
            m_ = p.do(eV, lambda e: e.tensor_max(tmpc[:], a2, a3), (), eV)
            m_ = p.do(eV, lambda e: e.tensor_tensor(
                thr[:], a2, a3, ALU.min), (), eV)
            m_ = p.do(eV, lambda e: e.tensor_tensor(
                tmpb[:], tmpb[:] if False else tmpb[:], thr[:], ALU.max)
                if False else e.tensor_max(keep[:], tmpb[:], thr[:]),
                (), eV)
            m_ = p.do(eV, lambda e: e.tensor_tensor(
                tmpb[:], tmpa[:], tmpc[:], ALU.min), (), eV)
            m_ = p.do(eV, lambda e: e.tensor_max(thr[:], tmpb[:], keep[:]),
                      (), eV)
            h1v = h1slab.ap().rearrange("p (n four) -> p n four", four=4)
            o1v = h1s_slab.ap().rearrange("p (n four) -> p n four", four=4)
            last = None
            for i4 in range(4):
                ai = sqv[:, :, i4]
                m_ = p.do(eV, lambda e, ai=ai: e.tensor_tensor(
                    keep[:], ai, thr[:], ALU.is_ge),
                    rb_h1s.use(0) if i4 == 0 else [], eV)
                last = p.do(eV, lambda e, i4=i4: e.tensor_mul(
                    o1v[:, :, i4], h1v[:, :, i4], keep[:]), (), eV)
            h14 = s_h1s.rearrange("(tt p) n -> p tt n", p=128)
            sst = p.dma("scalar", h14[:, :, ns * 256:(ns + 1) * 256],
                        h1s_slab.ap().rearrange("p (tt n) -> p tt n", n=256),
                        "h1st", waits=[last])
            rb_h1s.mark(0, sst)
            rb_slab.mark(0, (eV, p.tot[eV]))

    p.barrier()
    if n_phases <= 4.5:
        if "allocbomb" in dbg:
            with contextlib.ExitStack() as ph:
                for nm, sh, dty in (("zh1sg", [128, 64 * 512], BF16),
                                    ("zwdq", [128, 64 * 512], BF16),
                                    ("zxq40", [128, 512], F32),
                                    ("zxq41", [128, 512], F32),
                                    ("zoutq0", [128, 512], F32),
                                    ("zoutq1", [128, 512], F32),
                                    ("zy2q", [128, 512], F32),
                                    ("zgbt", [128, D], BF16),
                                    ("zmseg4", [1, D], BF16),
                                    ("zmseg4b", [1, D], BF16)):
                    ph.enter_context(nc.sbuf_tensor(nm, sh, dty))
                for i in range(2):
                    ph.enter_context(nc.psum_tensor(f"zps4o{i}", [128, 512],
                                                    F32))
        _finish(p, nc)
        return nc
    with contextlib.ExitStack() as ph:
        def psb(n_, s_, d_):
            return ph.enter_context(nc.sbuf_tensor(n_, s_, d_))

        def pps(n_, s_, d_=F32):
            return ph.enter_context(nc.psum_tensor(n_, s_, d_))

        h1sg = psb("h1sg", [128, 64 * 512], BF16)
        wdq = psb("wdq", [128, 64 * 512], BF16)
        xq4 = [psb(f"xq4{i}", [128, 512], F32) for i in range(2)]
        outq = [psb(f"outq{i}", [128, 512], F32) for i in range(2)]
        y2q = psb("y2q", [128, 512], F32)
        gbt = psb("gbt", [128, D], BF16)
        mseg = psb("mseg4", [1, D], BF16)
        mseg2 = psb("mseg4b", [1, D], BF16)
        ps_o = [pps(f"ps4o{i}", [128, 512]) for i in range(2)]
        ps_m4 = ps_o[0]

        vmod4 = None
        if "novm4" not in dbg:
            mld = p.dma("sync", mseg[:], s_mod[0:1, 5 * D:6 * D], "m4h")
            mld2 = p.dma("sync", mseg2[:], s_mod[1:2, 5 * D:6 * D], "m4l")
            prev = None
            for blk in range(4):
                off = blk * 512
                w0 = ([prev] if prev else []) + [mld, mld2]
                p.do(eT, lambda e, off=off: e.matmul(
                    ps_m4[:], ones1b[0:1, :], mseg[0:1, off:off + 512],
                    start=True, stop=False), w0, eT)
                tm = p.do(eT, lambda e, off=off: e.matmul(
                    ps_m4[:], ones1b[0:1, :], mseg2[0:1, off:off + 512],
                    start=False, stop=True), (), eT)
                prev = p.do(eV, lambda e, blk=blk: e.tensor_copy(
                    gbt[:, blk * 512:(blk + 1) * 512], ps_m4[:]), [tm], eV)
            vmod4 = prev

        rb_wd = Rot(1)
        rb_x4 = Rot(2)
        rb_oq = Rot(2)
        rb_pso = Rot(2)
        rb_h1g = Rot(1)
        for cq in range(4):
            c0 = cq * 512
            wld = None
            wlds = []
            if "nowd4" not in dbg:
                for dch in range(64):
                    wld = p.dma("gpsimd",
                                wdq[:, dch * 512:(dch + 1) * 512],
                                wdown[dch * 128:(dch + 1) * 128,
                                      c0:c0 + 512],
                                f"wdq{dch % 2}",
                                waits=(rb_wd.use(0) if dch < 2 else []))
                    wlds = wlds[-1:] + [wld]
                wld = None
            for g2 in range(4):
                t0 = g2 * 512
                gld = None
                glds = []
                if "nold4" not in dbg:
                    for dch in range(64):
                        gld = p.dmat("sync",
                                     h1sg[:, dch * 512:(dch + 1) * 512],
                                     s_h1s[t0:t0 + 512,
                                           dch * 128:(dch + 1) * 128],
                                     f"h1ld{dch % 4}",
                                     waits=(rb_h1g.use(0) if dch < 4
                                            else []) +
                                     ([("h1st", p.tot["h1st"])]
                                      if cq == 0 and g2 == 0 and dch < 4
                                      else []))
                        glds = glds[-3:] + [gld]
                    gld = None
                if "nomm4" in dbg:
                    continue
                for tt_ in range(4):
                    ti = g2 * 4 + tt_
                    pyb = ps_o[(g2 * 4 + tt_) % 2]
                    mm_w = wlds + glds + \
                        [w for w in (vmod4,) if w is not None] + \
                        rb_pso.use(g2 * 4 + tt_)
                    for dch in range(64):
                        p.do(eT, lambda e, pyb=pyb, dch=dch, tt_=tt_:
                             e.matmul(
                                 pyb[:],
                                 h1sg[:, dch * 512 + tt_ * 128:
                                      dch * 512 + tt_ * 128 + 128],
                                 wdq[:, dch * 512:(dch + 1) * 512],
                                 start=(dch == 0), stop=(dch == 63)),
                             mm_w if dch == 0 else (), eT)
                    ymm = (eT, p.tot[eT])
                    xb = xq4[tt_ % 2]
                    mx = p.dma("sync", xb[:],
                               s_x2[ti * 128:(ti + 1) * 128, c0:c0 + 512],
                               f"x4{tt_ % 2}",
                               waits=rb_x4.use(tt_)
                               + ([(f"x2st{tt_ % 2}",
                                    p.tot[f"x2st{tt_ % 2}"])]
                                  if cq == 0 and g2 == 0 else []))
                    m1_ = p.do(eV, lambda e, pyb=pyb, c0=c0:
                               e.tensor_mul(y2q[:], pyb[:],
                                            gbt[:, c0:c0 + 512]),
                               [ymm, vmod4], eV)
                    oq = outq[tt_ % 2]
                    m2_ = p.do(eV, lambda e, xb=xb, oq=oq:
                               e.tensor_add(oq[:], y2q[:], xb[:]),
                               [mx] + rb_oq.use(tt_), eV)
                    o_tgt = s_dum if "nost" in dbg else out
                    so_ = p.dma("scalar",
                                o_tgt[ti * 128:(ti + 1) * 128, c0:c0 + 512],
                                oq[:], f"outst{tt_ % 2}", waits=[m2_])
                    rb_pso.mark(g2 * 4 + tt_, m1_)
                    rb_x4.mark(tt_, m2_)
                    rb_oq.mark(tt_, so_)
                rb_h1g.mark(0, (eT, p.tot[eT]))
            rb_wd.mark(0, (eT, p.tot[eT]))

    if "x2h2" in dbg:
        w_ = [(s, p.tot[s]) for s in ("x2st0", "x2st1", "h2st0", "h2st1",
                                      "h1st") if s in p.tot]
        p.dma("gpsimd", dbg_t["x2"], s_x2[:, :], "dbg", waits=w_)
        p.dma("gpsimd", dbg_t["h2"], s_h2[:, :], "dbg", waits=w_)
        p.dma("gpsimd", dbg_t["h1s"], s_h1s[:, :], "dbg", waits=w_)
    _finish(p, nc)
    return nc


def _finish(p, nc):
    p.drain_all()
    with nc.Block() as block:
        p.replay(block)
    p.close()


def _consts():
    cst = np.zeros((128, NCST), np.float32)
    cst[:, C_INV_D] = 1.0 / 2048.0
    cst[:, C_EPS_LN] = 1e-5
    cst[:, C_NEGHALF] = -0.5
    cst[:, C_INV_DV] = 1.0 / 128.0
    cst[:, C_EPS_RMS] = 1e-6
    cst[:, C_ONE] = 1.0
    cst[:, C_L2EPS] = 1e-24
    cst[:, C_QTR] = 0.25
    cst[:, C_NEGINV_D] = -1.0 / 2048.0
    cst[:, C_NEGONE] = -1.0
    a = np.arange(128)
    m1 = np.where(a[None, :] > a[:, None], -1e30, 0.0)
    m2 = np.where(a[None, :] <= a[:, None], -1e30, 0.0)
    m3 = np.where(a[None, :] < a[:, None], -1e30, 0.0)
    g1 = np.where(a[None, :] < a[:, None], -1.0, 0.0)
    g2 = np.where(a[None, :] > a[:, None], -1.0, 0.0)
    g3 = np.where(a[None, :] >= a[:, None], 1.0, 0.0)
    eye32 = np.zeros((128, 1024), np.float32)
    for j in range(32):
        eye32[:, j * 32 + j] = 1.0
    sel32 = np.zeros((32, 32 * 128), np.float32)
    for j in range(32):
        sel32[j, j * 128:(j + 1) * 128] = 1.0
    return dict(
        cst=cst,
        identb=np.eye(128, dtype=BF),
        identf=np.eye(128, dtype=np.float32),
        ones1b=np.ones((1, 128), BF),
        ones32=np.ones((32, 128), np.float32),
        eye32=eye32.astype(BF),
        maskw=np.concatenate([m1, m2, m3], 1).astype(BF),
        gmask=np.concatenate([g1, g2, g3], 1).astype(BF),
        sel32=sel32.astype(BF),
    )


def make_in_maps(inputs):
    i = {k: np.asarray(v) for k, v in inputs.items()}
    asc = np.ascontiguousarray
    shared = dict(
        wmod=asc(i["L0_wmod"].astype(np.float32)),
        wqkvT=asc(i["L0_attn_W_qkv"].astype(np.float32).T),
        wzT=asc(i["L0_attn_W_z"].astype(np.float32).T),
        wbaT=asc(np.concatenate([i["L0_attn_W_b"], i["L0_attn_W_a"]], 0)
                 .astype(np.float32).T),
        convw=asc(i["L0_attn_conv_w"].astype(np.float32)
                  .reshape(64, 128, 4).transpose(1, 0, 2).reshape(128, 256)),
        dtb_al=asc(np.concatenate([i["L0_attn_dt_bias"],
                                   i["L0_attn_A_log"]])
                   .astype(np.float32)[None, :]),
        normw=asc(i["L0_attn_norm_w"].astype(np.float32)[None, :]),
        woutT=asc(i["L0_attn_W_out"].astype(np.float32).T),
        wgate=asc(i["L0_ffn_w_gate"].astype(np.float32)),
        wup=asc(i["L0_ffn_w_up"].astype(np.float32)),
        wdown=asc(i["L0_ffn_w_down"].astype(np.float32)),
        **_consts(),
    )
    maps = []
    for b in range(8):
        m = dict(shared)
        m["x"] = asc(i["L0_attn_x"][b].astype(np.float32))
        m["c_pf"] = asc(np.asarray(i["L0_c"][b], np.float32)
                        .reshape(16, 128).T)
        maps.append(m)
    return maps


def kernel(**inputs):
    nc = build_program()
    maps = make_in_maps(inputs)
    res = run_bass_kernel_spmd(nc, maps, core_ids=list(range(8)))
    return np.stack([np.asarray(r["out"], np.float32) for r in res.results])
